# version 14
# baseline (speedup 1.0000x reference)
import os
import numpy as np
import ml_dtypes
from contextlib import ExitStack

import concourse.bass as bass
import concourse.mybir as mybir
from concourse.bass_utils import run_bass_kernel_spmd
import concourse.dve_ops as dve_ops
from concourse.dve_op import DveOp
from concourse.dve_spec import Spec, Src0, Src1, C0, C1, C2, Zero, maxx, minn, select, lower
from concourse.dve_table_gen import dve_ver_for

F32, BF16 = mybir.dt.float32, mybir.dt.bfloat16
AF = mybir.ActivationFunctionType
ALU = mybir.AluOpType
AX = mybir.AxisListType

B, T, C, II = 4, 8192, 1024, 2816
GS = [[0, 3, 4, 7, 8, 11, 12, 15], [1, 2, 5, 6, 9, 10, 13, 14]]
NJ = [2 * (j + 1) for j in range(8)]
NCHUNK_COLS = 4 * sum(NJ)
M_RNE = 1.5 * 2.0 ** 22
C_VELT = 2.0 ** 22 + 1.0
NEG_BIG = -3.0e38
EPS = 1e-5


def _qlow_ref(in0, in1, s0, s1, imm2):
    t = np.clip(np.asarray(in0), -2.0, 2.0)
    return (np.round(t * 2.0) * 0.5).astype(np.asarray(in0).dtype)


def _qhi_ref(in0, in1, s0, s1, imm2):
    t = np.asarray(in0)
    a = np.abs(t)
    with np.errstate(divide="ignore", invalid="ignore"):
        e = np.where(a > 0, np.floor(np.log2(np.maximum(a, 1e-300))), 0.0)
    step = np.exp2(np.maximum(e - 1, -126.0))
    v = np.round(t / step) * step
    return np.where(a > 2.0, v, np.asarray(in1)).astype(t.dtype)


def _register_custom_ops():
    have = {op.name for op in dve_ops.OPS}
    ver = dve_ver_for("TRN2")
    out = {}
    _tv = Src0 * C1
    specs = [
        ("ATH_QLOW", Spec(body=(minn(maxx(Src0, C1), C0) + C2) - C2), _qlow_ref),
        ("ATH_QHI", Spec(body=select(maxx(Src0, Zero - Src0) > C0,
                                     _tv - (_tv - Src0), Src1)),
         _qhi_ref),
    ]
    for name, spec, ref in specs:
        if name in have:
            out[name] = next(op for op in dve_ops.OPS if op.name == name)
            continue
        low = lower(spec, ver=ver, name=name)
        op = DveOp(name, spec, subdim=False, uops_sha={ver: low.sha(ver)},
                   reference=ref)
        dve_ops.OPS.append(op)
        dve_ops.CUSTOM_DVE_REFS[op.name] = op.ref
        dve_ops.CUSTOM_DVE_SPECS[op.name] = spec
        dve_ops._SUB_OPCODE_FOR_NAME[op.name] = (
            dve_ops._CUSTOM_DVE_ROW_BASE + len(dve_ops.OPS) - 1)
        out[name] = op
    return out


_OPS = _register_custom_ops()
QLOW, QHI = _OPS["ATH_QLOW"], _OPS["ATH_QHI"]


class Ctr:
    def __init__(self, nc, name):
        self.h = nc.semaphore(name).__enter__()
        self.v = 0


def bcast16(ap_small, p, nblk):
    return ap_small.rearrange("p (b o) -> p b o", o=1).broadcast_to([p, nblk, 16])



def build_program(phases=(0, 1, 2, 3), dbg=()):
    nc = bass.Bass("TRN2", target_bir_lowering=False,
                   detect_race_conditions=False)

    def skind(name):
        return "ExternalOutput" if name in dbg else "Internal"

    X_SEQ = nc.dram_tensor("x_seq", [T, C], F32, kind="ExternalInput").ap()
    X_OWN = nc.dram_tensor("x_own", [T // 2, C], F32, kind="ExternalInput").ap()
    ME = nc.dram_tensor("me", [128, NCHUNK_COLS // 2], F32, kind="ExternalInput").ap()
    IDENT = nc.dram_tensor("ident", [128, 128], BF16, kind="ExternalInput").ap()
    WATTN = nc.dram_tensor("wattn", [C, 3 * C], F32, kind="ExternalInput").ap()
    BATTN = nc.dram_tensor("battn", [3 * C], F32, kind="ExternalInput").ap()
    BATTN_PM = nc.dram_tensor("battn_pm", [128, 24], F32, kind="ExternalInput").ap()
    WPROJ = nc.dram_tensor("wproj", [C, C], F32, kind="ExternalInput").ap()
    BPROJ = nc.dram_tensor("bproj", [C], F32, kind="ExternalInput").ap()
    WG = nc.dram_tensor("wg", [II, C], F32, kind="ExternalInput").ap()
    WU = nc.dram_tensor("wu", [II, C], F32, kind="ExternalInput").ap()
    WD = nc.dram_tensor("wd", [C, II], F32, kind="ExternalInput").ap()

    OUT = nc.dram_tensor("out_own", [T // 2, C], F32, kind="ExternalOutput").ap()

    KT_D = nc.dram_tensor("kt_d", [128, 8, T], BF16, kind=skind("kt_d")).ap()
    V_D = nc.dram_tensor("v_d", [T, C], BF16, kind=skind("v_d")).ap()
    QT_D = nc.dram_tensor("qt_d", [128, 8, T // 2], BF16, kind=skind("qt_d")).ap()
    XQT_D = nc.dram_tensor("xqt_d", [128, 8, T // 2], BF16, kind=skind("xqt_d")).ap()
    PART_D = nc.dram_tensor("part_d", [T // 2, C], F32, kind=skind("part_d")).ap()
    WGT_D = nc.dram_tensor("wgt_d", [128, 8, II], BF16, kind=skind("wgt_d")).ap()
    if "p2probe" in dbg:
        PRB_S = nc.dram_tensor("prb_s", [128, 2, 512], F32, kind="ExternalOutput").ap()
        PRB_P = nc.dram_tensor("prb_p", [128, 2, 512], BF16, kind="ExternalOutput").ap()
        PRB_ML = nc.dram_tensor("prb_ml", [128, 16], F32, kind="ExternalOutput").ap()
        PRB_O = nc.dram_tensor("prb_o", [128, C], F32, kind="ExternalOutput").ap()
        PRB_PT = nc.dram_tensor("prb_pt", [128, 8, 128], BF16, kind="ExternalOutput").ap()
    WUT_D = nc.dram_tensor("wut_d", [128, 8, II], BF16, kind=skind("wut_d")).ap()
    WDT_D = nc.dram_tensor("wdt_d", [128, 22, C], BF16, kind=skind("wdt_d")).ap()

    all_ctrs = []

    def mkc(name):
        c = Ctr(nc, name)
        all_ctrs.append(c)
        return c

    sT, sV, sS = mkc("sT"), mkc("sV"), mkc("sS")
    dL = mkc("dL")

    eng = {"T": nc.tensor, "V": nc.vector, "S": nc.scalar,
           "g": nc.gpsimd, "L": nc.sync}

    def W(e, ctr, v=None):
        eng[e].wait_ge(ctr.h, ctr.v if v is None else v)

    def tV(ins):
        ins.then_inc(sV.h, 1)
        sV.v += 1
        return sV.v

    def tS(ins):
        ins.then_inc(sS.h, 1)
        sS.v += 1
        return sS.v

    def tT(ins):
        ins.then_inc(sT.h, 1)
        sT.v += 1
        return sT.v

    def dma(e, ctr, out, in_):
        eng[e].dma_start(out=out, in_=in_).then_inc(ctr.h, 16)
        ctr.v += 16
        return ctr.v

    def barrier():
        for e in ("T", "V", "S", "g", "L"):
            for ctr in all_ctrs:
                if ctr.v:
                    eng[e].wait_ge(ctr.h, ctr.v)

    es_global = ExitStack()
    sbg = lambda n, s, d=F32: es_global.enter_context(nc.sbuf_tensor(n, s, d))

    ident_b = sbg("ident_b", [128, 128], BF16)
    ones_b = sbg("ones_b", [1, 128], BF16)
    vdmy = sbg("vdmy", [128, 8])
    sdmy = sbg("sdmy", [128, 8])

    dma("L", dL, ident_b[:], IDENT)
    W("V", dL)
    tV(nc.vector.memset(ones_b[:], 1.0))

    def vspace():
        nc.vector.tensor_copy(vdmy[:], ident_b[:, 0:8])

    def sspace():
        nc.scalar.copy(sdmy[:], ident_b[:, 0:8])

    def emit_qdq(src_ap, dst_ap, bufs, Wd):
        nb = Wd // 16
        v = nc.vector
        src3 = src_ap.rearrange("p (b k) -> p b k", k=16)
        bm, bmr, rr, sc = (bufs[k] for k in ("bm", "bmr", "rr", "sc"))
        tq, ql, qg = (bufs[k] for k in ("tq", "ql", "qg"))
        vspace()
        v.tensor_reduce(bm[:, :nb], src3, AX.X, ALU.max, apply_absolute_value=True)
        vspace()
        v.reciprocal(bmr[:, :nb], bm[:, :nb])
        vspace()
        v.tensor_scalar(rr[:, :nb], bmr[:, :nb], 6.0, 1e12, ALU.mult, ALU.min)
        vspace()
        v.reciprocal(sc[:, :nb], rr[:, :nb])
        t3 = tq[:, :Wd].rearrange("p (b k) -> p b k", k=16)
        v.tensor_tensor(t3, src3, bcast16(rr[:, :nb], 128, nb), ALU.mult)
        vspace()
        v._custom_dve(QLOW, out=ql[:, :Wd], in0=tq[:, :Wd], s0=2.0, s1=-2.0,
                      imm2=M_RNE)
        vspace()
        v._custom_dve(QHI, out=qg[:, :Wd], in0=tq[:, :Wd], in1=ql[:, :Wd],
                      s0=2.0, s1=C_VELT)
        vspace()
        q3 = qg[:, :Wd].rearrange("p (b k) -> p b k", k=16)
        d3 = dst_ap.rearrange("p (b k) -> p b k", k=16)
        return v.tensor_tensor(d3, q3, bcast16(sc[:, :nb], 128, nb), ALU.mult)

    if 0 in phases or 1 in phases:
        es1 = ExitStack()
        sb1 = lambda n, s, d=F32: es1.enter_context(nc.sbuf_tensor(n, s, d))

        if 0 in phases:
            WH = 1024
            wld = sb1("wld", [128, WH])
            qbufs0 = {"tq": sb1("w_tq", [128, WH]), "ql": sb1("w_ql", [128, WH]),
                      "qg": sb1("w_qg", [128, WH]), "bm": sb1("w_bm", [128, WH // 16]),
                      "bmr": sb1("w_bmr", [128, WH // 16]),
                      "rr": sb1("w_rr", [128, WH // 16]),
                      "sc": sb1("w_sc", [128, WH // 16]),
                      "dmy": sb1("w_dmy", [128, 1])}
            wqb = sb1("wqb", [128, WH], BF16)
            wstage = sb1("wstage", [128, 8, 128], BF16)
            psW = es1.enter_context(nc.psum_tensor("psW", [128, 8, 128], BF16))

            dWL = mkc("dWL")
            dWS = mkc("dWS")
            jobs = []
            for wsrc, wdst, nch, Wd in ((WG, WGT_D, 22, C), (WU, WUT_D, 22, C),
                                        (WD, WDT_D, 8, II)):
                for i in range(nch):
                    for c0 in range(0, Wd, WH):
                        jobs.append((wsrc, wdst, i, c0, min(WH, Wd - c0)))

            p0_state = {"wld_free": None, "wstage_free": None, "jn": 0}

            def emit_p0_job(wsrc, wdst, i, c0, Wd):
                jn = p0_state["jn"]
                p0_state["jn"] += 1
                if p0_state["wld_free"] is not None:
                    W("g", sV, p0_state["wld_free"])
                dma("g", dWL, wld[:, :Wd],
                    wsrc[i * 128:(i + 1) * 128, c0:c0 + Wd])
                W("V", dWL)
                if p0_state.get("wqb_free") is not None:
                    W("V", sT, p0_state["wqb_free"])
                p0_state["wld_free"] = tV(emit_qdq(wld[:, :Wd], wqb[:, :Wd],
                                                   qbufs0, Wd))
                ntp = Wd // 128
                ch0 = c0 // 128
                W("T", sV)
                for g0 in range(0, ntp, 8):
                    gn = min(8, ntp - g0)
                    if jn > 0 or g0 > 0:
                        W("T", sS)
                    for k in range(gn):
                        ins = nc.tensor.transpose(
                            psW[:, k, :], wqb[:, (g0 + k) * 128:(g0 + k + 1) * 128],
                            ident_b[:])
                    tT(ins)
                    W("S", sT)
                    if g0 == 0 and p0_state["wstage_free"] is not None:
                        W("S", dWS)
                    tS(nc.scalar.copy(wstage[:, g0:g0 + gn, :], psW[:, 0:gn, :]))
                W("g", sS)
                for g0 in range(0, ntp, 8):
                    gn = min(8, ntp - g0)
                    dma("g", dWS, wdst[:, ch0 + g0:ch0 + g0 + gn,
                                       i * 128:(i + 1) * 128],
                        wstage[:, g0:g0 + gn, :])
                p0_state["wstage_free"] = dWS.v
                p0_state["wqb_free"] = sT.v

            p0_queue = list(jobs)
            if 1 not in phases:
                for job in p0_queue:
                    emit_p0_job(*job)
                p0_queue = []

        if 1 in phases:
            wat_b = sb1("wat_b", [128, 8, 3 * C], BF16)
            battn_sb = sb1("battn_sb", [128, 24])
            bv_b = sb1("bv_b", [1, C], BF16)

            dWA = [mkc("dWA0"), mkc("dWA1")]
            dBB = mkc("dBB")
            with ExitStack() as es_wa:
                wat_f = [es_wa.enter_context(nc.sbuf_tensor(f"wat_f{i}",
                                                            [128, 3 * C], F32))
                         for i in range(2)]
                bv_f = es_wa.enter_context(nc.sbuf_tensor("bv_f", [1, C], F32))
                for fc in range(8):
                    par = fc % 2
                    if fc >= 2:
                        W("L", sS)
                    dma("L", dWA[par], wat_f[par][:],
                        WATTN[fc * 128:(fc + 1) * 128, :])
                    W("S", dWA[par])
                    tS(nc.scalar.copy(wat_b[:, fc, :], wat_f[par][:]))
                dma("L", dBB, battn_sb[:], BATTN_PM)
                dma("L", dBB, bv_f[:],
                    BATTN[2 * C:3 * C].rearrange("(o f) -> o f", o=1))
                W("S", dBB)
                tS(nc.scalar.copy(bv_b[:], bv_f[:]))
            s_prologue_done = sS.v

            epsv = sb1("epsv", [128, 1])
            W("V", dBB)
            tV(nc.vector.memset(epsv[:], EPS))
            xsc = [sb1(f"xsc{i}", [128, 4, C]) for i in range(2)]
            scr = sb1("scr", [128, C])
            sum4 = sb1("sum4", [128, 4])
            ssq4 = sb1("ssq4", [128, 4])
            mean4 = sb1("mean4", [128, 4])
            ms24 = sb1("ms24", [128, 4])
            var4 = sb1("var4", [128, 4])
            lnv4 = sb1("lnv4", [128, 4])
            rstd4 = sb1("rstd4", [128, 4])
            nm4 = sb1("nm4", [128, 4])
            xnf = sb1("xnf", [128, C])
            xnb = [sb1(f"xnb{i}", [128, C], BF16) for i in range(4)]
            xnT = [sb1(f"xnT{i}", [128, 8, 512], BF16) for i in range(2)]
            kst = sb1("kst", [128, 8, 512], BF16)
            vst = sb1("vst", [128, 4, C], BF16)
            qst = kst
            xqst = vst[:].rearrange("p c (a b) -> p (c a) b", b=512)
            xqb = [sb1(f"xqb{i}", [128, C], BF16) for i in range(4)]
            if 0 in phases:
                qbufs1 = dict(qbufs0)
            else:
                qbufs1 = {"tq": sb1("p1_tq", [128, C]), "ql": sb1("p1_ql", [128, C]),
                          "qg": sb1("p1_qg", [128, C]),
                          "bm": sb1("p1_bm", [128, 64]),
                          "bmr": sb1("p1_bmr", [128, 64]),
                          "rr": sb1("p1_rr", [128, 64]),
                          "sc": sb1("p1_sc", [128, 64]),
                          "dmy": sb1("p1_dmy", [128, 1])}
            psT2 = [es1.enter_context(nc.psum_tensor(f"psT{i}", [128, 8, 128], BF16))
                    for i in range(2)]
            psK = [es1.enter_context(nc.psum_tensor(f"psK{i}", [128, 512], F32))
                   for i in range(2)]
            psV2 = [es1.enter_context(nc.psum_tensor(f"psV{i}", [128, 512], F32))
                    for i in range(2)]

            dX = [mkc("dX0"), mkc("dX1")]
            dKS = mkc("dKS")
            dVS = mkc("dVS")
            free_x = [None, None]
            free_xnf = [None]
            tok_xqtr = [None, None, None, None]
            free_xnb = [None, None, None, None]
            free_psT = [None, None]
            free_xnT = [None, None]
            free_psK = [None, None]
            free_psV = [None, None]

            def ln_sv(sidx, src_rows_ap, do_q):
                par = sidx % 2
                if sidx == 0:
                    W("L", sS, s_prologue_done)
                if free_x[par] is not None:
                    W("L", sS, free_x[par])
                dma("L", dX[par], xsc[par][:],
                    src_rows_ap.rearrange("(c p) f -> p c f", p=128))
                W("S", dX[par])
                for c in range(4):
                    ins = nc.scalar.activation(scr[:], xsc[par][:, c, :], AF.Square,
                                               accum_out=ssq4[:, c:c + 1])
                tS(ins)
                W("V", dX[par])
                nc.vector.tensor_reduce(sum4[:], xsc[par][:],
                                        AX.X, ALU.add)
                vspace()
                W("V", sS)
                nc.vector.tensor_scalar_mul(mean4[:], sum4[:], 1.0 / C)
                vspace()
                nc.vector.tensor_tensor(ms24[:], mean4[:], mean4[:], ALU.mult)
                vspace()
                tV(nc.vector.scalar_tensor_tensor(var4[:], ssq4[:], 1.0 / C,
                                                  ms24[:], ALU.mult, ALU.subtract))
                W("S", sV)
                nc.scalar.activation(lnv4[:], var4[:], AF.Ln, bias=epsv[:])
                sspace()
                tS(nc.scalar.activation(rstd4[:], lnv4[:], AF.Exp, scale=-0.5))
                W("V", sS)
                v_nm = tV(nc.vector.scalar_tensor_tensor(
                    nm4[:], mean4[:], -1.0, rstd4[:], ALU.mult, ALU.mult))

                toks = []
                for c in range(4):
                    W("S", sV, v_nm)
                    if free_xnf[0] is not None:
                        W("S", sV, free_xnf[0])
                    s_xnf = tS(nc.scalar.activation(
                        xnf[:], xsc[par][:, c, :], AF.Identity,
                        scale=rstd4[:, c:c + 1], bias=nm4[:, c:c + 1]))
                    if c == 3:
                        free_x[par] = s_xnf
                    W("V", sS, s_xnf)
                    if free_xnb[c] is not None:
                        W("V", sT, free_xnb[c])
                    v_xnb = tV(nc.vector.tensor_copy(xnb[c][:], xnf[:]))
                    if do_q:
                        if tok_xqtr[c] is not None:
                            W("V", sT, tok_xqtr[c])
                        tV(emit_qdq(xnf[:], xqb[c][:], qbufs1, C))
                    free_xnf[0] = sV.v
                    toks.append((v_xnb, sV.v))
                return toks

            def ln_t(par, do_q, r0, toks):
                for c in range(4):
                    cp = c % 2
                    v_xnb, v_xq = toks[c]
                    W("T", sV, v_xnb)
                    if free_psT[cp] is not None:
                        W("T", sS if free_psT[cp][0] == "S" else sV,
                          free_psT[cp][1])
                    for k in range(8):
                        ins = nc.tensor.transpose(
                            psT2[cp][:, k, :], xnb[c][:, k * 128:(k + 1) * 128],
                            ident_b[:])
                    tT(ins)
                    free_xnb[c] = sT.v
                    W("S", sT)
                    if free_xnT[par] is not None and c == 0:
                        W("S", sT, free_xnT[par])
                    free_psT[cp] = ("S", tS(nc.scalar.copy(
                        xnT[par][:, :, c * 128:(c + 1) * 128], psT2[cp][:])))
                    if do_q:
                        op = (c + 1) % 2
                        W("T", sV, v_xq)
                        if free_psT[op] is not None:
                            W("T", sS if free_psT[op][0] == "S" else sV,
                              free_psT[op][1])
                        for k in range(8):
                            ins = nc.tensor.transpose(
                                psT2[op][:, k, :], xqb[c][:, k * 128:(k + 1) * 128],
                                ident_b[:])
                        tT(ins)
                        tok_xqtr[c] = sT.v
                        W("S", sT)
                        if dVS.v and c == 0:
                            W("S", dVS)
                        free_psT[op] = ("S", tS(nc.scalar.copy(
                            xqst[:, :, c * 128:(c + 1) * 128], psT2[op][:])))
                if do_q:
                    dma("S", dVS, XQT_D[:, :, r0:r0 + 512], xqst)
                return (sS.v, sV.v)

            def mm_part(par, do_kv, do_q, r0, ts):
                s_xnT_done_s, s_xnT_done_v = ts
                W("T", sS, s_xnT_done_s)
                W("T", sV, s_xnT_done_v)
                if do_kv:
                    for mc in range(8):
                        mp = mc % 2
                        if free_psK[mp] is not None:
                            W("T", sS if free_psK[mp][0] == "S" else sV,
                              free_psK[mp][1])
                        for fc in range(8):
                            ins = nc.tensor.matmul(
                                psK[mp][:],
                                wat_b[:, fc, C + mc * 128:C + (mc + 1) * 128],
                                xnT[par][:, fc, :], start=(fc == 0), stop=(fc == 7))
                        tT(ins)
                        W("S", sT)
                        if mc == 0 and dKS.v:
                            W("S", dKS)
                        ev = tS(nc.scalar.activation(
                            kst[:, mc, :], psK[mp][:], AF.Identity,
                            bias=battn_sb[:, 8 + mc:9 + mc]))
                        free_psK[mp] = ("S", ev)
                    dma("S", dKS, KT_D[:, :, r0:r0 + 512], kst[:])
                    for c in range(4):
                        for h in range(2):
                            vp = (c * 2 + h) % 2
                            if free_psV[vp] is not None:
                                W("T", sS, free_psV[vp])
                            for fc in range(8):
                                nc.tensor.matmul(
                                    psV2[vp][:],
                                    xnT[par][:, fc, c * 128:(c + 1) * 128],
                                    wat_b[:, fc, 2 * C + h * 512:2 * C + (h + 1) * 512],
                                    start=(fc == 0), stop=False)
                            tT(nc.tensor.matmul(psV2[vp][:], ones_b[:],
                                                bv_b[:, h * 512:(h + 1) * 512],
                                                start=False, stop=True))
                            W("S", sT)
                            if c == 0 and h == 0 and dVS.v:
                                W("S", dVS)
                            free_psV[vp] = tS(nc.scalar.copy(
                                vst[:, c, h * 512:(h + 1) * 512], psV2[vp][:]))
                    dma("S", dVS, V_D[r0:r0 + 512, :].rearrange(
                        "(c p) f -> p c f", p=128), vst[:])
                if do_q:
                    for mc in range(8):
                        mp = mc % 2
                        if free_psK[mp] is not None:
                            W("T", sS if free_psK[mp][0] == "S" else sV,
                              free_psK[mp][1])
                        for fc in range(8):
                            ins = nc.tensor.matmul(
                                psK[mp][:], wat_b[:, fc, mc * 128:(mc + 1) * 128],
                                xnT[par][:, fc, :], start=(fc == 0), stop=(fc == 7))
                        tT(ins)
                        W("S", sT)
                        if mc == 0 and dKS.v:
                            W("S", dKS)
                        ev = tS(nc.scalar.activation(
                            qst[:, mc, :], psK[mp][:], AF.Identity,
                            bias=battn_sb[:, mc:mc + 1]))
                        free_psK[mp] = ("S", ev)
                    dma("S", dKS, QT_D[:, :, r0:r0 + 512], qst[:])
                free_xnT[par] = sT.v

            sctr = 0
            nper = 5 if 0 in phases else 0
            pend_mm = None
            for S_ in range(16):
                toks = ln_sv(sctr, X_SEQ[S_ * 512:(S_ + 1) * 512, :], do_q=False)
                if pend_mm is not None:
                    mm_part(**pend_mm)
                ts = ln_t(sctr % 2, False, S_ * 512, toks)
                pend_mm = dict(par=sctr % 2, do_kv=True, do_q=False,
                               r0=S_ * 512, ts=ts)
                sctr += 1
                if 0 in phases:
                    for job in p0_queue[:nper]:
                        emit_p0_job(*job)
                    p0_queue = p0_queue[nper:]
            if 0 in phases:
                for job in p0_queue:
                    emit_p0_job(*job)
                p0_queue = []
            for O_ in range(8):
                toks = ln_sv(sctr, X_OWN[O_ * 512:(O_ + 1) * 512, :], do_q=True)
                if pend_mm is not None:
                    mm_part(**pend_mm)
                ts = ln_t(sctr % 2, True, O_ * 512, toks)
                pend_mm = dict(par=sctr % 2, do_kv=False, do_q=True,
                               r0=O_ * 512, ts=ts)
                sctr += 1
            mm_part(**pend_mm)
            pend_mm = None

        barrier()
        es1.close()

    if 2 in phases:
        es2 = ExitStack()
        sb2 = lambda n, s, d=F32: es2.enter_context(nc.sbuf_tensor(n, s, d))

        wproj_f = sb2("wproj_f", [128, C])
        wproj_b = sb2("wproj_b", [128, 8, C], BF16)
        bp_f = sb2("bp_f", [1, C])
        bp_b = sb2("bp_b", [1, C], BF16)
        mep_sb = sb2("mep_sb", [128, NCHUNK_COLS // 2])
        qt = [sb2(f"qt{i}", [128, 8, 512], BF16) for i in range(2)]
        ktg = [sb2(f"ktg{i}", [128, 8, 1024], BF16) for i in range(2)]
        vg = [sb2(f"vg{i}", [128, 8, 1024], BF16) for i in range(2)]
        ssb = sb2("ssb", [128, 2, 512])
        pb = sb2("pb", [128, 4, 512], BF16)
        pt = sb2("pt", [128, 8, 128], BF16)
        oacc = sb2("oacc", [128, 4, C])
        mrun = [sb2(f"mrun{i}", [128, 4]) for i in range(2)]
        lrun = sb2("lrun", [128, 4])
        nmn = sb2("nmn", [128, 4])
        alpha = sb2("alpha", [128, 4])
        mtmp = sb2("mtmp", [128, 1])
        mtmp2 = sb2("mtmp2", [128, 1])
        dmy2 = sb2("dmy2", [128, 1])
        psum_e = sb2("psum_e", [128, 2])
        rl = sb2("rl", [128, 1])
        onb = sb2("onb", [128, C], BF16)
        ont = sb2("ont", [128, 8, 128], BF16)
        xres = [sb2(f"xres{i}", [128, C]) for i in range(2)]
        part = [sb2(f"part{i}", [128, C]) for i in range(2)]

        psS2 = [es2.enter_context(nc.psum_tensor(f"psS{i}", [128, 1024], F32))
                for i in range(2)]
        psPT2 = [es2.enter_context(nc.psum_tensor(f"psPT{i}", [128, 8, 128], BF16))
                 for i in range(2)]
        psO = es2.enter_context(nc.psum_tensor("psO", [128, 1024], F32))

        dWP = mkc("dWP")
        dME = mkc("dME")
        dKT = [mkc("dKT0"), mkc("dKT1")]
        dVG = [mkc("dVG0"), mkc("dVG1")]
        dQT = [mkc("dQT0"), mkc("dQT1")]
        dXR = [mkc("dXR0"), mkc("dXR1")]
        dPART = [mkc("dPART0"), mkc("dPART1")]
        for fc in range(8):
            if fc >= 1:
                W("L", sS)
            dma("L", dWP, wproj_f[:], WPROJ[fc * 128:(fc + 1) * 128, :])
            W("S", dWP)
            tS(nc.scalar.copy(wproj_b[:, fc, :], wproj_f[:]))
        dma("L", dME, bp_f[:], BPROJ.rearrange("(o f) -> o f", o=1))
        dma("L", dME, mep_sb[:], ME)
        W("S", dME)
        tS(nc.scalar.copy(bp_b[:], bp_f[:]))
        W("V", dME)
        nc.vector.memset(oacc[:], 0.0)
        tV(nc.vector.memset(lrun[:], 0.0))

        free_qt = [None, None]
        free_ktg = [None, None]
        free_vg = [None, None]
        part_tok = [None, None]

        dma("L", dQT[0], qt[0][:], QT_D[:, :, 0:512])

        gctr = 0
        tok_mask2 = [None, None]
        tok_merge = None
        tok_ptcopy = [None, None]
        pend2 = None
        grp_free = {}

        def flush_pend():
            nonlocal pend2, tok_merge
            psub, pgp, ptok_exp = pend2
            pend2 = None
            pp = psub % 2
            W("T", sS, ptok_exp)
            if tok_ptcopy[pp] is not None:
                W("T", sS, tok_ptcopy[pp])
            for t in range(2):
                for k4 in range(4):
                    ins = nc.tensor.transpose(
                        psPT2[pp][:, t * 4 + k4, :],
                        pb[:, pp * 2 + t, k4 * 128:(k4 + 1) * 128], ident_b[:])
            tT(ins)
            W("S", sT)
            tok_ptcopy[pp] = tS(nc.scalar.copy(pt[:], psPT2[pp][:]))
            W("T", sS, tok_ptcopy[pp])
            W("T", dVG[pgp])
            if tok_merge is not None:
                W("T", sV, tok_merge)
            for h in range(2):
                for kc in range(8):
                    ins = nc.tensor.matmul(
                        psO[:, h * 512:(h + 1) * 512], pt[:, kc, :],
                        vg[pgp][:, kc, h * 512:(h + 1) * 512],
                        start=(kc == 0), stop=(kc == 7))
                tT(ins)
            free_ktg[pgp] = sT.v
            free_vg[pgp] = sT.v
            W("V", sT)
            tok_merge = tV(nc.vector.scalar_tensor_tensor(
                oacc[:, psub, :], oacc[:, psub, :], alpha[:, psub:psub + 1],
                psO[:], ALU.mult, ALU.add))

        for j in range(8):
            jp = j % 2
            ngrp = NJ[j] // 2
            if j < 7:
                if free_qt[(j + 1) % 2] is not None:
                    W("L", sT, free_qt[(j + 1) % 2])
                dma("L", dQT[(j + 1) % 2], qt[(j + 1) % 2][:],
                    QT_D[:, :, (j + 1) * 512:(j + 2) * 512])
            W("V", sS)
            tV(nc.vector.memset(mrun[0][:], NEG_BIG))

            for g in range(ngrp):
                gp = gctr % 2
                if free_ktg[gp] is not None:
                    W("L", sT, free_ktg[gp])
                dma("L", dKT[gp], ktg[gp][:],
                    KT_D[:, :, g * 1024:(g + 1) * 1024])
                if free_vg[gp] is not None:
                    nc.scalar.wait_ge(sT.h, free_vg[gp])
                dma("S", dVG[gp], vg[gp][:],
                    V_D[g * 1024:(g + 1) * 1024, :]
                    .rearrange("(kc p) f -> p kc f", p=128))

                ml_old = mrun[g % 2]
                ml_new = mrun[(g + 1) % 2]
                for sub in range(4):
                    pp = sub % 2
                    W("T", dKT[gp])
                    if g == 0 and sub == 0:
                        W("T", dQT[jp])
                    if tok_mask2[pp] is not None:
                        W("T", sV, tok_mask2[pp])
                    for t in range(2):
                        for fc in range(8):
                            ins = nc.tensor.matmul(
                                psS2[pp][:, t * 512:(t + 1) * 512],
                                qt[jp][:, fc, sub * 128:(sub + 1) * 128],
                                ktg[gp][:, fc, t * 512:(t + 1) * 512],
                                start=(fc == 0), stop=(fc == 7))
                        tT(ins)
                    W("V", sT)
                    colp = 2 * j * (j + 1) + g * 4 + sub
                    nc.vector.tensor_mask_reduce(
                        ssb[:], psS2[pp][:], 0.0, mep_sb[:, colp:colp + 1],
                        1.0 / 32.0, NEG_BIG, ALU.max, accum_out=mtmp[:])
                    vspace()
                    nc.vector.tensor_max(ml_new[:, sub:sub + 1],
                                         mtmp[:], ml_old[:, sub:sub + 1])
                    vspace()
                    tok_mask2[pp] = tV(nc.vector.tensor_scalar_mul(
                        nmn[:, sub:sub + 1], ml_new[:, sub:sub + 1], -1.0))
                    W("S", sV)
                    nc.scalar.activation(alpha[:, sub:sub + 1],
                                         ml_old[:, sub:sub + 1], AF.Exp,
                                         bias=nmn[:, sub:sub + 1])
                    tok_exp = tS(nc.scalar.activation(
                        pb[:, pp * 2:pp * 2 + 2, :].rearrange("p a b -> p (a b)"),
                        ssb[:].rearrange("p a b -> p (a b)"), AF.Exp,
                        bias=nmn[:, sub:sub + 1], accum_out=psum_e[:, 0:1]))
                    W("V", sS)
                    tV(nc.vector.scalar_tensor_tensor(
                        lrun[:, sub:sub + 1], lrun[:, sub:sub + 1],
                        alpha[:, sub:sub + 1], psum_e[:, 0:1], ALU.mult, ALU.add))
                    if pend2 is not None:
                        flush_pend()
                    pend2 = (sub, gp, tok_exp)
                gctr += 1

            if pend2 is not None:
                flush_pend()
            for sub in range(4):
                rows = j * 512 + sub * 128
                xp = sub % 2
                if j > 0 or sub > 1:
                    W("L", sV)
                dma("L", dXR[xp], xres[xp][:], X_OWN[rows:rows + 128, :])
                W("V", sS)
                nc.vector.reciprocal(rl[:], lrun[:, sub:sub + 1])
                tV(nc.vector.tensor_copy(dmy2[:], lrun[:, 0:1]))
                W("S", sV)
                tS(nc.scalar.activation(onb[:], oacc[:, sub, :], AF.Identity,
                                        scale=rl[:]))
                W("T", sS)
                if tok_ptcopy[0] is not None:
                    W("T", sS, tok_ptcopy[0])
                for k in range(8):
                    ins = nc.tensor.transpose(psPT2[0][:, k, :],
                                              onb[:, k * 128:(k + 1) * 128],
                                              ident_b[:])
                tT(ins)
                W("S", sT)
                tok_ptcopy[0] = tS(nc.scalar.copy(ont[:], psPT2[0][:]))
                W("T", sS)
                if tok_merge is not None:
                    W("T", sV, tok_merge)
                for h in range(2):
                    for fc in range(8):
                        nc.tensor.matmul(
                            psO[:, h * 512:(h + 1) * 512], ont[:, fc, :],
                            wproj_b[:, fc, h * 512:(h + 1) * 512],
                            start=(fc == 0), stop=False)
                    tT(nc.tensor.matmul(psO[:, h * 512:(h + 1) * 512], ones_b[:],
                                        bp_b[:, h * 512:(h + 1) * 512],
                                        start=False, stop=True))
                W("V", sT)
                W("V", dXR[xp])
                if part_tok[xp] is not None:
                    W("V", dPART[xp])
                tok_merge = tV(nc.vector.tensor_add(part[xp][:], xres[xp][:],
                                                    psO[:]))
                W("g", sV)
                dma("g", dPART[xp], PART_D[rows:rows + 128, :], part[xp][:])
                part_tok[xp] = dPART[xp].v
            free_qt[jp] = sT.v

        barrier()
        es2.close()

    if 3 in phases:
        es3 = ExitStack()
        sb3 = lambda n, s, d=F32: es3.enter_context(nc.sbuf_tensor(n, s, d))

        wgt = sb3("wgt", [128, 8, II], BF16)
        wut = sb3("wut", [128, 8, II], BF16)
        wdt = sb3("wdt", [128, 22, C], BF16)
        xqt = [sb3(f"xqt{i}", [128, 8, 512], BF16) for i in range(2)]
        sg = sb3("sg", [128, 512])
        hb = sb3("hb", [128, 512])
        hq = sb3("hq", [128, 512], BF16)
        hqt = sb3("hqt", [128, 4, 128], BF16)
        qb3 = {"tq": sb3("p3_tq", [128, 512]), "ql": sb3("p3_ql", [128, 512]),
               "qg": sb3("p3_qg", [128, 512]), "bm": sb3("p3_bm", [128, 32]),
               "bmr": sb3("p3_bmr", [128, 32]), "rr": sb3("p3_rr", [128, 32]),
               "sc": sb3("p3_sc", [128, 32]), "dmy": sb3("p3_dmy", [128, 1])}
        pres = [sb3(f"pres{i}", [128, C]) for i in range(2)]
        outb = [sb3(f"outb{i}", [128, C]) for i in range(2)]

        psG = [es3.enter_context(nc.psum_tensor(f"psG{i}", [128, 512], F32))
               for i in range(2)]
        psU = [es3.enter_context(nc.psum_tensor(f"psU{i}", [128, 512], F32))
               for i in range(2)]
        psHT = es3.enter_context(nc.psum_tensor("psHT", [128, 4, 128], BF16))
        psD = es3.enter_context(nc.psum_tensor("psD", [128, 1024], F32))

        dW3 = mkc("dW3")
        dXQT = [mkc("dXQT0"), mkc("dXQT1")]
        dPR = [mkc("dPR0"), mkc("dPR1")]
        dOUT = [mkc("dOUT0"), mkc("dOUT1")]
        dma("L", dW3, wgt[:], WGT_D[:])
        dma("L", dW3, wut[:], WUT_D[:])
        dma("L", dW3, wdt[:], WDT_D[:])

        ic_w = [512, 512, 512, 512, 512, 256]
        tokG = [None, None]
        tokU = [None, None]
        tok_hqt = None
        hq_free = None
        tok_out = None
        free_xqt = [None, None]
        free_pres = [None, None]
        free_outb = [True, True]

        for tt in range(32):
            tp = (tt // 4) % 2
            ts4 = tt % 4
            if ts4 == 0:
                if free_xqt[tp] is not None:
                    W("L", sT, free_xqt[tp])
                dma("L", dXQT[tp], xqt[tp][:],
                    XQT_D[:, :, tt * 128:(tt + 4) * 128])
            if free_pres[tp] is not None:
                nc.scalar.wait_ge(sV.h, free_pres[tp])
            dma("S", dPR[tp], pres[tp][:], PART_D[tt * 128:(tt + 1) * 128, :])

            def emit_tail(pic, pwic, ptokq):
                nonlocal tok_hqt, hq_free
                ntp = pwic // 128
                W("T", sV, ptokq)
                if tok_hqt is not None:
                    W("T", sS, tok_hqt)
                for k in range(ntp):
                    ins = nc.tensor.transpose(psHT[:, k, :],
                                              hq[:, k * 128:(k + 1) * 128],
                                              ident_b[:])
                tT(ins)
                hq_free = sT.v
                W("S", sT)
                tok_hqt = tS(nc.scalar.copy(hqt[:, 0:ntp, :], psHT[:, 0:ntp, :]))
                W("T", sS, tok_hqt)
                if pic == 0 and tok_out is not None:
                    W("T", sV, tok_out)
                for qc in range(ntp):
                    for h in range(2):
                        ins = nc.tensor.matmul(
                            psD[:, h * 512:(h + 1) * 512], hqt[:, qc, :],
                            wdt[:, pic * 4 + qc, h * 512:(h + 1) * 512],
                            start=(pic == 0 and qc == 0),
                            stop=(pic == 5 and qc == ntp - 1))
                tT(ins)

            i0 = 0
            pend = None
            for ic in range(6):
                wic = ic_w[ic]
                gp = ic % 2
                W("T", dW3)
                W("T", dXQT[tp])
                if tokG[gp] is not None:
                    W("T", sS, tokG[gp])
                if tokU[gp] is not None:
                    W("T", sV, tokU[gp])
                for fc in range(8):
                    ins = nc.tensor.matmul(
                        psG[gp][:, 0:wic],
                        xqt[tp][:, fc, ts4 * 128:(ts4 + 1) * 128],
                        wgt[:, fc, i0:i0 + wic],
                        start=(fc == 0), stop=(fc == 7))
                tT(ins)
                tok_gmm = sT.v
                for fc in range(8):
                    ins = nc.tensor.matmul(
                        psU[gp][:, 0:wic],
                        xqt[tp][:, fc, ts4 * 128:(ts4 + 1) * 128],
                        wut[:, fc, i0:i0 + wic],
                        start=(fc == 0), stop=(fc == 7))
                tT(ins)
                tok_umm = sT.v
                if pend is not None:
                    emit_tail(*pend)
                W("S", sT, tok_gmm)
                tokG[gp] = tS(nc.scalar.activation(sg[:, 0:wic], psG[gp][:, 0:wic],
                                                   AF.Silu))
                W("V", sS, tokG[gp])
                W("V", sT, tok_umm)
                if hq_free is not None:
                    W("V", sT, hq_free)
                tokU[gp] = tV(nc.vector.tensor_tensor(hb[:, 0:wic], sg[:, 0:wic],
                                                      psU[gp][:, 0:wic], ALU.mult))
                tokq = tV(emit_qdq(hb[:, 0:wic], hq[:, 0:wic], qb3, wic))
                pend = (ic, wic, tokq)
                i0 += wic
            emit_tail(*pend)
            pend = None
            if ts4 == 3:
                free_xqt[tp] = sT.v
            W("V", sT)
            W("V", dPR[tp])
            if dOUT[tp].v:
                W("V", dOUT[tp])
            tok_out = tV(nc.vector.tensor_add(outb[tp][:], pres[tp][:],
                                              psD[:]))
            free_pres[tp] = tok_out
            W("g", sV)
            dma("g", dOUT[tp], OUT[tt * 128:(tt + 1) * 128, :], outb[tp][:])

        nc.gpsimd.wait_ge(dOUT[0].h, dOUT[0].v)
        nc.gpsimd.wait_ge(dOUT[1].h, dOUT[1].v)
        es3.close()

    es_global.close()
    return nc



def _build_me(ty):
    me = np.zeros((128, NCHUNK_COLS // 2), dtype=np.float32)
    p = np.arange(128, dtype=np.float32)
    col = 0
    for j in range(8):
        g = GS[ty][j]
        for gg in range(NJ[j] // 2):
            for sub in range(4):
                me[:, col] = np.maximum(
                    (512.0 * g + 128.0 * sub + p + 1.0) - 1024.0 * gg, 0.0)
                col += 1
    assert col == NCHUNK_COLS // 2
    return me


def kernel(L0_attn_x, L0_attn_w_attn, L0_attn_b_attn, L0_attn_w_proj,
           L0_attn_b_proj, L0_ffn_w_gate, L0_ffn_w_up, L0_ffn_w_down):
    x = np.ascontiguousarray(np.asarray(L0_attn_x, dtype=np.float32))
    wattn = np.ascontiguousarray(np.asarray(L0_attn_w_attn, dtype=np.float32))
    battn = np.ascontiguousarray(np.asarray(L0_attn_b_attn, dtype=np.float32))
    wproj = np.ascontiguousarray(np.asarray(L0_attn_w_proj, dtype=np.float32))
    bproj = np.ascontiguousarray(np.asarray(L0_attn_b_proj, dtype=np.float32))
    wg = np.ascontiguousarray(np.asarray(L0_ffn_w_gate, dtype=np.float32))
    wu = np.ascontiguousarray(np.asarray(L0_ffn_w_up, dtype=np.float32))
    wd = np.ascontiguousarray(np.asarray(L0_ffn_w_down, dtype=np.float32))
    ident = np.eye(128, dtype=np.float32).astype(ml_dtypes.bfloat16)
    battn_pm = np.ascontiguousarray(battn.reshape(24, 128).T)
    mes = [_build_me(0), _build_me(1)]

    in_maps = []
    for core in range(8):
        s, ty = core // 2, core % 2
        x_own = np.ascontiguousarray(
            np.concatenate([x[s][512 * g:512 * (g + 1)] for g in GS[ty]], axis=0))
        in_maps.append({
            "x_seq": x[s], "x_own": x_own, "me": mes[ty], "ident": ident,
            "wattn": wattn, "battn": battn, "battn_pm": battn_pm,
            "wproj": wproj, "bproj": bproj,
            "wg": wg, "wu": wu, "wd": wd,
        })

    nc = build_program()
    mybir.codegen_inst_isa_subclasses(nc)
    res = run_bass_kernel_spmd(nc, in_maps, core_ids=list(range(8)))

    out = np.empty((B, T, C), dtype=np.float32)
    for core in range(8):
        s, ty = core // 2, core % 2
        oo = np.asarray(res.results[core]["out_own"])
        for j, g in enumerate(GS[ty]):
            out[s][512 * g:512 * (g + 1)] = oo[512 * j:512 * (j + 1)]
    return out


# version 15
# speedup vs baseline: 1.1974x; 1.1974x over previous
import os
import numpy as np
import ml_dtypes
from contextlib import ExitStack

import concourse.bass as bass
import concourse.mybir as mybir
from concourse.bass_utils import run_bass_kernel_spmd
import concourse.dve_ops as dve_ops
from concourse.dve_op import DveOp
from concourse.dve_spec import Spec, Src0, Src1, C0, C1, C2, Zero, maxx, minn, select, lower
from concourse.dve_table_gen import dve_ver_for

F32, BF16 = mybir.dt.float32, mybir.dt.bfloat16
AF = mybir.ActivationFunctionType
ALU = mybir.AluOpType
AX = mybir.AxisListType

B, T, C, II = 4, 8192, 1024, 2816
GS = [[0, 3, 4, 7, 8, 11, 12, 15], [1, 2, 5, 6, 9, 10, 13, 14]]
NJ = [2 * (j + 1) for j in range(8)]
NCHUNK_COLS = 4 * sum(NJ)
M_RNE = 1.5 * 2.0 ** 22
C_VELT = 2.0 ** 22 + 1.0
NEG_BIG = -3.0e38
EPS = 1e-5


def _qlow_ref(in0, in1, s0, s1, imm2):
    t = np.clip(np.asarray(in0), -2.0, 2.0)
    return (np.round(t * 2.0) * 0.5).astype(np.asarray(in0).dtype)


def _qhi_ref(in0, in1, s0, s1, imm2):
    t = np.asarray(in0)
    a = np.abs(t)
    with np.errstate(divide="ignore", invalid="ignore"):
        e = np.where(a > 0, np.floor(np.log2(np.maximum(a, 1e-300))), 0.0)
    step = np.exp2(np.maximum(e - 1, -126.0))
    v = np.round(t / step) * step
    return np.where(a > 2.0, v, np.asarray(in1)).astype(t.dtype)


def _register_custom_ops():
    have = {op.name for op in dve_ops.OPS}
    ver = dve_ver_for("TRN2")
    out = {}
    _tv = Src0 * C1
    specs = [
        ("ATH_QLOW", Spec(body=(minn(maxx(Src0, C1), C0) + C2) - C2), _qlow_ref),
        ("ATH_QHI", Spec(body=select(maxx(Src0, Zero - Src0) > C0,
                                     _tv - (_tv - Src0), Src1)),
         _qhi_ref),
    ]
    for name, spec, ref in specs:
        if name in have:
            out[name] = next(op for op in dve_ops.OPS if op.name == name)
            continue
        low = lower(spec, ver=ver, name=name)
        op = DveOp(name, spec, subdim=False, uops_sha={ver: low.sha(ver)},
                   reference=ref)
        dve_ops.OPS.append(op)
        dve_ops.CUSTOM_DVE_REFS[op.name] = op.ref
        dve_ops.CUSTOM_DVE_SPECS[op.name] = spec
        dve_ops._SUB_OPCODE_FOR_NAME[op.name] = (
            dve_ops._CUSTOM_DVE_ROW_BASE + len(dve_ops.OPS) - 1)
        out[name] = op
    return out


_OPS = _register_custom_ops()
QLOW, QHI = _OPS["ATH_QLOW"], _OPS["ATH_QHI"]


class Ctr:
    def __init__(self, nc, name):
        self.h = nc.semaphore(name).__enter__()
        self.v = 0


def bcast16(ap_small, p, nblk):
    return ap_small.rearrange("p (b o) -> p b o", o=1).broadcast_to([p, nblk, 16])



def build_program(phases=(0, 1, 2, 3), dbg=()):
    nc = bass.Bass("TRN2", target_bir_lowering=False,
                   detect_race_conditions=False)

    def skind(name):
        return "ExternalOutput" if name in dbg else "Internal"

    X_SEQ = nc.dram_tensor("x_seq", [T, C], F32, kind="ExternalInput").ap()
    X_OWN = nc.dram_tensor("x_own", [T // 2, C], F32, kind="ExternalInput").ap()
    ME = nc.dram_tensor("me", [128, NCHUNK_COLS // 2], F32, kind="ExternalInput").ap()
    IDENT = nc.dram_tensor("ident", [128, 128], BF16, kind="ExternalInput").ap()
    WATTN = nc.dram_tensor("wattn", [C, 3 * C], F32, kind="ExternalInput").ap()
    BATTN = nc.dram_tensor("battn", [3 * C], F32, kind="ExternalInput").ap()
    BATTN_PM = nc.dram_tensor("battn_pm", [128, 24], F32, kind="ExternalInput").ap()
    WPROJ = nc.dram_tensor("wproj", [C, C], F32, kind="ExternalInput").ap()
    BPROJ = nc.dram_tensor("bproj", [C], F32, kind="ExternalInput").ap()
    WG = nc.dram_tensor("wg", [II, C], F32, kind="ExternalInput").ap()
    WU = nc.dram_tensor("wu", [II, C], F32, kind="ExternalInput").ap()
    WD = nc.dram_tensor("wd", [C, II], F32, kind="ExternalInput").ap()

    OUT = nc.dram_tensor("out_own", [T // 2, C], F32, kind="ExternalOutput").ap()

    KT_D = nc.dram_tensor("kt_d", [128, 8, T], BF16, kind=skind("kt_d")).ap()
    V_D = nc.dram_tensor("v_d", [T, C], BF16, kind=skind("v_d")).ap()
    QT_D = nc.dram_tensor("qt_d", [128, 8, T // 2], BF16, kind=skind("qt_d")).ap()
    XQT_D = nc.dram_tensor("xqt_d", [128, 8, T // 2], BF16, kind=skind("xqt_d")).ap()
    PART_D = nc.dram_tensor("part_d", [T // 2, C], F32, kind=skind("part_d")).ap()
    WGT_D = nc.dram_tensor("wgt_d", [128, 8, II], BF16, kind=skind("wgt_d")).ap()
    if "p2probe" in dbg:
        PRB_S = nc.dram_tensor("prb_s", [128, 2, 512], F32, kind="ExternalOutput").ap()
        PRB_P = nc.dram_tensor("prb_p", [128, 2, 512], BF16, kind="ExternalOutput").ap()
        PRB_ML = nc.dram_tensor("prb_ml", [128, 16], F32, kind="ExternalOutput").ap()
        PRB_O = nc.dram_tensor("prb_o", [128, C], F32, kind="ExternalOutput").ap()
        PRB_PT = nc.dram_tensor("prb_pt", [128, 8, 128], BF16, kind="ExternalOutput").ap()
    WUT_D = nc.dram_tensor("wut_d", [128, 8, II], BF16, kind=skind("wut_d")).ap()
    WDT_D = nc.dram_tensor("wdt_d", [128, 22, C], BF16, kind=skind("wdt_d")).ap()

    all_ctrs = []

    def mkc(name):
        c = Ctr(nc, name)
        all_ctrs.append(c)
        return c

    sT, sV, sS = mkc("sT"), mkc("sV"), mkc("sS")
    dL = mkc("dL")

    eng = {"T": nc.tensor, "V": nc.vector, "S": nc.scalar,
           "g": nc.gpsimd, "L": nc.sync}

    def W(e, ctr, v=None):
        eng[e].wait_ge(ctr.h, ctr.v if v is None else v)

    def tV(ins):
        ins.then_inc(sV.h, 1)
        sV.v += 1
        return sV.v

    def tS(ins):
        ins.then_inc(sS.h, 1)
        sS.v += 1
        return sS.v

    def tT(ins):
        ins.then_inc(sT.h, 1)
        sT.v += 1
        return sT.v

    def dma(e, ctr, out, in_):
        eng[e].dma_start(out=out, in_=in_).then_inc(ctr.h, 16)
        ctr.v += 16
        return ctr.v

    def barrier():
        for e in ("T", "V", "S", "g", "L"):
            for ctr in all_ctrs:
                if ctr.v:
                    eng[e].wait_ge(ctr.h, ctr.v)

    es_global = ExitStack()
    sbg = lambda n, s, d=F32: es_global.enter_context(nc.sbuf_tensor(n, s, d))

    ident_b = sbg("ident_b", [128, 128], BF16)
    ones_b = sbg("ones_b", [1, 128], BF16)
    vdmy = sbg("vdmy", [128, 8])
    sdmy = sbg("sdmy", [128, 8])

    dma("L", dL, ident_b[:], IDENT)
    W("V", dL)
    tV(nc.vector.memset(ones_b[:], 1.0))

    def vspace():
        nc.vector.tensor_copy(vdmy[:], ident_b[:, 0:8])

    def sspace():
        nc.scalar.copy(sdmy[:], ident_b[:, 0:8])

    def emit_qdq(src_ap, dst_ap, bufs, Wd):
        nb = Wd // 16
        v = nc.vector
        src3 = src_ap.rearrange("p (b k) -> p b k", k=16)
        bm, bmr, rr, sc = (bufs[k] for k in ("bm", "bmr", "rr", "sc"))
        tq, ql, qg = (bufs[k] for k in ("tq", "ql", "qg"))
        vspace()
        v.tensor_reduce(bm[:, :nb], src3, AX.X, ALU.max, apply_absolute_value=True)
        vspace()
        v.reciprocal(bmr[:, :nb], bm[:, :nb])
        vspace()
        v.tensor_scalar(rr[:, :nb], bmr[:, :nb], 6.0, 1e12, ALU.mult, ALU.min)
        vspace()
        v.reciprocal(sc[:, :nb], rr[:, :nb])
        t3 = tq[:, :Wd].rearrange("p (b k) -> p b k", k=16)
        v.tensor_tensor(t3, src3, bcast16(rr[:, :nb], 128, nb), ALU.mult)
        vspace()
        v._custom_dve(QLOW, out=ql[:, :Wd], in0=tq[:, :Wd], s0=2.0, s1=-2.0,
                      imm2=M_RNE)
        vspace()
        v._custom_dve(QHI, out=qg[:, :Wd], in0=tq[:, :Wd], in1=ql[:, :Wd],
                      s0=2.0, s1=C_VELT)
        vspace()
        q3 = qg[:, :Wd].rearrange("p (b k) -> p b k", k=16)
        d3 = dst_ap.rearrange("p (b k) -> p b k", k=16)
        return v.tensor_tensor(d3, q3, bcast16(sc[:, :nb], 128, nb), ALU.mult)

    if 0 in phases or 1 in phases:
        es1 = ExitStack()
        sb1 = lambda n, s, d=F32: es1.enter_context(nc.sbuf_tensor(n, s, d))

        if 0 in phases:
            WH = 1024
            wld = sb1("wld", [128, WH])
            qbufs0 = {"tq": sb1("w_tq", [128, WH]), "ql": sb1("w_ql", [128, WH]),
                      "qg": sb1("w_qg", [128, WH]), "bm": sb1("w_bm", [128, WH // 16]),
                      "bmr": sb1("w_bmr", [128, WH // 16]),
                      "rr": sb1("w_rr", [128, WH // 16]),
                      "sc": sb1("w_sc", [128, WH // 16]),
                      "dmy": sb1("w_dmy", [128, 1])}
            wqb = sb1("wqb", [128, WH], BF16)
            wstage = sb1("wstage", [128, 8, 128], BF16)
            psW = es1.enter_context(nc.psum_tensor("psW", [128, 8, 128], BF16))

            dWL = mkc("dWL")
            dWS = mkc("dWS")
            jobs = []
            for wsrc, wdst, nch, Wd in ((WG, WGT_D, 22, C), (WU, WUT_D, 22, C),
                                        (WD, WDT_D, 8, II)):
                for i in range(nch):
                    for c0 in range(0, Wd, WH):
                        jobs.append((wsrc, wdst, i, c0, min(WH, Wd - c0)))

            p0_state = {"wld_free": None, "wstage_free": None, "jn": 0}

            def emit_p0_job(wsrc, wdst, i, c0, Wd):
                jn = p0_state["jn"]
                p0_state["jn"] += 1
                if p0_state["wld_free"] is not None:
                    W("L", sV, p0_state["wld_free"])
                dma("L", dWL, wld[:, :Wd],
                    wsrc[i * 128:(i + 1) * 128, c0:c0 + Wd])
                W("V", dWL)
                if p0_state.get("wqb_free") is not None:
                    W("V", sT, p0_state["wqb_free"])
                p0_state["wld_free"] = tV(emit_qdq(wld[:, :Wd], wqb[:, :Wd],
                                                   qbufs0, Wd))
                ntp = Wd // 128
                ch0 = c0 // 128
                W("T", sV)
                for g0 in range(0, ntp, 8):
                    gn = min(8, ntp - g0)
                    if jn > 0 or g0 > 0:
                        W("T", sS)
                    for k in range(gn):
                        ins = nc.tensor.transpose(
                            psW[:, k, :], wqb[:, (g0 + k) * 128:(g0 + k + 1) * 128],
                            ident_b[:])
                    tT(ins)
                    W("S", sT)
                    if g0 == 0 and p0_state["wstage_free"] is not None:
                        W("S", dWS)
                    tS(nc.scalar.copy(wstage[:, g0:g0 + gn, :], psW[:, 0:gn, :]))
                W("g", sS)
                for g0 in range(0, ntp, 8):
                    gn = min(8, ntp - g0)
                    dma("g", dWS, wdst[:, ch0 + g0:ch0 + g0 + gn,
                                       i * 128:(i + 1) * 128],
                        wstage[:, g0:g0 + gn, :])
                p0_state["wstage_free"] = dWS.v
                p0_state["wqb_free"] = sT.v

            p0_queue = list(jobs)
            if 1 not in phases:
                for job in p0_queue:
                    emit_p0_job(*job)
                p0_queue = []

        if 1 in phases:
            wat_b = sb1("wat_b", [128, 8, 3 * C], BF16)
            battn_sb = sb1("battn_sb", [128, 24])
            bv_b = sb1("bv_b", [1, C], BF16)

            dWA = [mkc("dWA0"), mkc("dWA1")]
            dBB = mkc("dBB")
            with ExitStack() as es_wa:
                wat_f = [es_wa.enter_context(nc.sbuf_tensor(f"wat_f{i}",
                                                            [128, 3 * C], F32))
                         for i in range(2)]
                bv_f = es_wa.enter_context(nc.sbuf_tensor("bv_f", [1, C], F32))
                for fc in range(8):
                    par = fc % 2
                    if fc >= 2:
                        W("L", sS)
                    dma("L", dWA[par], wat_f[par][:],
                        WATTN[fc * 128:(fc + 1) * 128, :])
                    W("S", dWA[par])
                    tS(nc.scalar.copy(wat_b[:, fc, :], wat_f[par][:]))
                dma("L", dBB, battn_sb[:], BATTN_PM)
                dma("L", dBB, bv_f[:],
                    BATTN[2 * C:3 * C].rearrange("(o f) -> o f", o=1))
                W("S", dBB)
                tS(nc.scalar.copy(bv_b[:], bv_f[:]))
            s_prologue_done = sS.v

            epsv = sb1("epsv", [128, 1])
            W("V", dBB)
            tV(nc.vector.memset(epsv[:], EPS))
            xsc = [sb1(f"xsc{i}", [128, 4, C]) for i in range(2)]
            scr = sb1("scr", [128, C])
            sum4 = sb1("sum4", [128, 4])
            ssq4 = sb1("ssq4", [128, 4])
            mean4 = sb1("mean4", [128, 4])
            ms24 = sb1("ms24", [128, 4])
            var4 = sb1("var4", [128, 4])
            lnv4 = sb1("lnv4", [128, 4])
            rstd4 = sb1("rstd4", [128, 4])
            nm4 = sb1("nm4", [128, 4])
            xnf = sb1("xnf", [128, C])
            xnb = [sb1(f"xnb{i}", [128, C], BF16) for i in range(4)]
            xnT = [sb1(f"xnT{i}", [128, 8, 512], BF16) for i in range(2)]
            kst = sb1("kst", [128, 8, 512], BF16)
            vst = sb1("vst", [128, 4, C], BF16)
            qst = kst
            xqst = vst[:].rearrange("p c (a b) -> p (c a) b", b=512)
            xqb = [sb1(f"xqb{i}", [128, C], BF16) for i in range(4)]
            if 0 in phases:
                qbufs1 = dict(qbufs0)
            else:
                qbufs1 = {"tq": sb1("p1_tq", [128, C]), "ql": sb1("p1_ql", [128, C]),
                          "qg": sb1("p1_qg", [128, C]),
                          "bm": sb1("p1_bm", [128, 64]),
                          "bmr": sb1("p1_bmr", [128, 64]),
                          "rr": sb1("p1_rr", [128, 64]),
                          "sc": sb1("p1_sc", [128, 64]),
                          "dmy": sb1("p1_dmy", [128, 1])}
            psT2 = [es1.enter_context(nc.psum_tensor(f"psT{i}", [128, 8, 128], BF16))
                    for i in range(2)]
            psK = [es1.enter_context(nc.psum_tensor(f"psK{i}", [128, 512], F32))
                   for i in range(2)]
            psV2 = [es1.enter_context(nc.psum_tensor(f"psV{i}", [128, 512], F32))
                    for i in range(2)]

            dX = [mkc("dX0"), mkc("dX1")]
            dKS = mkc("dKS")
            dVS = mkc("dVS")
            free_x = [None, None]
            free_xnf = [None]
            tok_xqtr = [None, None, None, None]
            free_xnb = [None, None, None, None]
            free_psT = [None, None]
            free_xnT = [None, None]
            free_psK = [None, None]
            free_psV = [None, None]

            def ln_sv(sidx, src_rows_ap, do_q):
                par = sidx % 2
                if sidx == 0:
                    W("L", sS, s_prologue_done)
                if free_x[par] is not None:
                    W("L", sS, free_x[par])
                dma("L", dX[par], xsc[par][:],
                    src_rows_ap.rearrange("(c p) f -> p c f", p=128))
                W("S", dX[par])
                for c in range(4):
                    ins = nc.scalar.activation(scr[:], xsc[par][:, c, :], AF.Square,
                                               accum_out=ssq4[:, c:c + 1])
                tS(ins)
                W("V", dX[par])
                nc.vector.tensor_reduce(sum4[:], xsc[par][:],
                                        AX.X, ALU.add)
                vspace()
                W("V", sS)
                nc.vector.tensor_scalar_mul(mean4[:], sum4[:], 1.0 / C)
                vspace()
                nc.vector.tensor_tensor(ms24[:], mean4[:], mean4[:], ALU.mult)
                vspace()
                tV(nc.vector.scalar_tensor_tensor(var4[:], ssq4[:], 1.0 / C,
                                                  ms24[:], ALU.mult, ALU.subtract))
                W("S", sV)
                nc.scalar.activation(lnv4[:], var4[:], AF.Ln, bias=epsv[:])
                sspace()
                tS(nc.scalar.activation(rstd4[:], lnv4[:], AF.Exp, scale=-0.5))
                W("V", sS)
                v_nm = tV(nc.vector.scalar_tensor_tensor(
                    nm4[:], mean4[:], -1.0, rstd4[:], ALU.mult, ALU.mult))

                toks = []
                for c in range(4):
                    W("S", sV, v_nm)
                    if free_xnf[0] is not None:
                        W("S", sV, free_xnf[0])
                    s_xnf = tS(nc.scalar.activation(
                        xnf[:], xsc[par][:, c, :], AF.Identity,
                        scale=rstd4[:, c:c + 1], bias=nm4[:, c:c + 1]))
                    if c == 3:
                        free_x[par] = s_xnf
                    W("V", sS, s_xnf)
                    if free_xnb[c] is not None:
                        W("V", sT, free_xnb[c])
                    v_xnb = tV(nc.vector.tensor_copy(xnb[c][:], xnf[:]))
                    if do_q:
                        if tok_xqtr[c] is not None:
                            W("V", sT, tok_xqtr[c])
                        tV(emit_qdq(xnf[:], xqb[c][:], qbufs1, C))
                    free_xnf[0] = sV.v
                    toks.append((v_xnb, sV.v))
                return toks

            def ln_t(par, do_q, r0, toks):
                for c in range(4):
                    cp = c % 2
                    v_xnb, v_xq = toks[c]
                    W("T", sV, v_xnb)
                    if free_psT[cp] is not None:
                        W("T", sS if free_psT[cp][0] == "S" else sV,
                          free_psT[cp][1])
                    for k in range(8):
                        ins = nc.tensor.transpose(
                            psT2[cp][:, k, :], xnb[c][:, k * 128:(k + 1) * 128],
                            ident_b[:])
                    tT(ins)
                    free_xnb[c] = sT.v
                    W("S", sT)
                    if free_xnT[par] is not None and c == 0:
                        W("S", sT, free_xnT[par])
                    free_psT[cp] = ("S", tS(nc.scalar.copy(
                        xnT[par][:, :, c * 128:(c + 1) * 128], psT2[cp][:])))
                    if do_q:
                        op = (c + 1) % 2
                        W("T", sV, v_xq)
                        if free_psT[op] is not None:
                            W("T", sS if free_psT[op][0] == "S" else sV,
                              free_psT[op][1])
                        for k in range(8):
                            ins = nc.tensor.transpose(
                                psT2[op][:, k, :], xqb[c][:, k * 128:(k + 1) * 128],
                                ident_b[:])
                        tT(ins)
                        tok_xqtr[c] = sT.v
                        W("S", sT)
                        if dVS.v and c == 0:
                            W("S", dVS)
                        free_psT[op] = ("S", tS(nc.scalar.copy(
                            xqst[:, :, c * 128:(c + 1) * 128], psT2[op][:])))
                if do_q:
                    W("g", sS)
                    dma("g", dVS, XQT_D[:, :, r0:r0 + 512], xqst)
                return (sS.v, sV.v)

            def mm_part(par, do_kv, do_q, r0, ts):
                s_xnT_done_s, s_xnT_done_v = ts
                W("T", sS, s_xnT_done_s)
                W("T", sV, s_xnT_done_v)
                if do_kv:
                    for mc in range(8):
                        mp = mc % 2
                        if free_psK[mp] is not None:
                            W("T", sS if free_psK[mp][0] == "S" else sV,
                              free_psK[mp][1])
                        for fc in range(8):
                            ins = nc.tensor.matmul(
                                psK[mp][:],
                                wat_b[:, fc, C + mc * 128:C + (mc + 1) * 128],
                                xnT[par][:, fc, :], start=(fc == 0), stop=(fc == 7))
                        tT(ins)
                        W("S", sT)
                        if mc == 0 and dKS.v:
                            W("S", dKS)
                        ev = tS(nc.scalar.activation(
                            kst[:, mc, :], psK[mp][:], AF.Identity,
                            bias=battn_sb[:, 8 + mc:9 + mc]))
                        free_psK[mp] = ("S", ev)
                    W("g", sS)
                    dma("g", dKS, KT_D[:, :, r0:r0 + 512], kst[:])
                    for c in range(4):
                        for h in range(2):
                            vp = (c * 2 + h) % 2
                            if free_psV[vp] is not None:
                                W("T", sS, free_psV[vp])
                            for fc in range(8):
                                nc.tensor.matmul(
                                    psV2[vp][:],
                                    xnT[par][:, fc, c * 128:(c + 1) * 128],
                                    wat_b[:, fc, 2 * C + h * 512:2 * C + (h + 1) * 512],
                                    start=(fc == 0), stop=False)
                            tT(nc.tensor.matmul(psV2[vp][:], ones_b[:],
                                                bv_b[:, h * 512:(h + 1) * 512],
                                                start=False, stop=True))
                            W("S", sT)
                            if c == 0 and h == 0 and dVS.v:
                                W("S", dVS)
                            free_psV[vp] = tS(nc.scalar.copy(
                                vst[:, c, h * 512:(h + 1) * 512], psV2[vp][:]))
                    W("g", sS)
                    dma("g", dVS, V_D[r0:r0 + 512, :].rearrange(
                        "(c p) f -> p c f", p=128), vst[:])
                if do_q:
                    for mc in range(8):
                        mp = mc % 2
                        if free_psK[mp] is not None:
                            W("T", sS if free_psK[mp][0] == "S" else sV,
                              free_psK[mp][1])
                        for fc in range(8):
                            ins = nc.tensor.matmul(
                                psK[mp][:], wat_b[:, fc, mc * 128:(mc + 1) * 128],
                                xnT[par][:, fc, :], start=(fc == 0), stop=(fc == 7))
                        tT(ins)
                        W("S", sT)
                        if mc == 0 and dKS.v:
                            W("S", dKS)
                        ev = tS(nc.scalar.activation(
                            qst[:, mc, :], psK[mp][:], AF.Identity,
                            bias=battn_sb[:, mc:mc + 1]))
                        free_psK[mp] = ("S", ev)
                    W("g", sS)
                    dma("g", dKS, QT_D[:, :, r0:r0 + 512], qst[:])
                free_xnT[par] = sT.v

            sctr = 0
            nper = 5 if 0 in phases else 0
            pend_mm = None
            for S_ in range(16):
                toks = ln_sv(sctr, X_SEQ[S_ * 512:(S_ + 1) * 512, :], do_q=False)
                if pend_mm is not None:
                    mm_part(**pend_mm)
                ts = ln_t(sctr % 2, False, S_ * 512, toks)
                pend_mm = dict(par=sctr % 2, do_kv=True, do_q=False,
                               r0=S_ * 512, ts=ts)
                sctr += 1
                if 0 in phases:
                    for job in p0_queue[:nper]:
                        emit_p0_job(*job)
                    p0_queue = p0_queue[nper:]
            if 0 in phases:
                for job in p0_queue:
                    emit_p0_job(*job)
                p0_queue = []
            for O_ in range(8):
                toks = ln_sv(sctr, X_OWN[O_ * 512:(O_ + 1) * 512, :], do_q=True)
                if pend_mm is not None:
                    mm_part(**pend_mm)
                ts = ln_t(sctr % 2, True, O_ * 512, toks)
                pend_mm = dict(par=sctr % 2, do_kv=False, do_q=True,
                               r0=O_ * 512, ts=ts)
                sctr += 1
            mm_part(**pend_mm)
            pend_mm = None

        barrier()
        es1.close()

    if 2 in phases:
        es2 = ExitStack()
        sb2 = lambda n, s, d=F32: es2.enter_context(nc.sbuf_tensor(n, s, d))

        wproj_f = sb2("wproj_f", [128, C])
        wproj_b = sb2("wproj_b", [128, 8, C], BF16)
        bp_f = sb2("bp_f", [1, C])
        bp_b = sb2("bp_b", [1, C], BF16)
        mep_sb = sb2("mep_sb", [128, NCHUNK_COLS // 2])
        qt = [sb2(f"qt{i}", [128, 8, 512], BF16) for i in range(2)]
        ktg = [sb2(f"ktg{i}", [128, 8, 1024], BF16) for i in range(2)]
        vg = [sb2(f"vg{i}", [128, 8, 1024], BF16) for i in range(2)]
        ssb = sb2("ssb", [128, 2, 512])
        pb = sb2("pb", [128, 4, 512], BF16)
        pt2 = [sb2(f"pt{i}", [128, 8, 128], BF16) for i in range(2)]
        oacc = sb2("oacc", [128, 4, C])
        mrun = [sb2(f"mrun{i}", [128, 4]) for i in range(2)]
        lrun = sb2("lrun", [128, 4])
        nmn = sb2("nmn", [128, 4])
        alpha = sb2("alpha", [128, 4])
        mtmp = sb2("mtmp", [128, 1])
        mtmp2 = sb2("mtmp2", [128, 1])
        dmy2 = sb2("dmy2", [128, 1])
        psum_e = sb2("psum_e", [128, 2])
        rl = sb2("rl", [128, 1])
        onb = sb2("onb", [128, C], BF16)
        ont = sb2("ont", [128, 8, 128], BF16)
        xres = [sb2(f"xres{i}", [128, C]) for i in range(2)]
        part = [sb2(f"part{i}", [128, C]) for i in range(2)]

        psS2 = [es2.enter_context(nc.psum_tensor(f"psS{i}", [128, 1024], F32))
                for i in range(2)]
        psPT2 = [es2.enter_context(nc.psum_tensor(f"psPT{i}", [128, 8, 128], BF16))
                 for i in range(2)]
        psO = es2.enter_context(nc.psum_tensor("psO", [128, 1024], F32))

        dWP = mkc("dWP")
        dME = mkc("dME")
        dKT = [mkc("dKT0"), mkc("dKT1")]
        dVG = [mkc("dVG0"), mkc("dVG1")]
        dQT = [mkc("dQT0"), mkc("dQT1")]
        dXR = [mkc("dXR0"), mkc("dXR1")]
        dPART = [mkc("dPART0"), mkc("dPART1")]
        for fc in range(8):
            if fc >= 1:
                W("L", sS)
            dma("L", dWP, wproj_f[:], WPROJ[fc * 128:(fc + 1) * 128, :])
            W("S", dWP)
            tS(nc.scalar.copy(wproj_b[:, fc, :], wproj_f[:]))
        dma("L", dME, bp_f[:], BPROJ.rearrange("(o f) -> o f", o=1))
        dma("L", dME, mep_sb[:], ME)
        W("S", dME)
        tS(nc.scalar.copy(bp_b[:], bp_f[:]))
        W("V", dME)
        nc.vector.memset(oacc[:], 0.0)
        tV(nc.vector.memset(lrun[:], 0.0))

        free_qt = [None, None]
        free_ktg = [None, None]
        free_vg = [None, None]
        part_tok = [None, None]

        dma("L", dQT[0], qt[0][:], QT_D[:, :, 0:512])

        gctr = 0
        tok_mask2 = [None, None]
        tok_merge = None
        tok_ptcopy = [None, None]
        pendA = None
        pendB = None

        def flushA(pend):
            psub, pgp, ptok_exp = pend
            pp = psub % 2
            W("T", sS, ptok_exp)
            if tok_ptcopy[pp] is not None:
                W("T", sS, tok_ptcopy[pp])
            for t in range(2):
                for k4 in range(4):
                    ins = nc.tensor.transpose(
                        psPT2[pp][:, t * 4 + k4, :],
                        pb[:, pp * 2 + t, k4 * 128:(k4 + 1) * 128], ident_b[:])
            tT(ins)
            W("S", sT)
            tok_ptcopy[pp] = tS(nc.scalar.copy(pt2[pp][:], psPT2[pp][:]))
            return (psub, pgp, tok_ptcopy[pp])

        def flushB(pend):
            nonlocal tok_merge
            psub, pgp, ptcopy = pend
            pp = psub % 2
            W("T", sS, ptcopy)
            W("T", dVG[pgp])
            if tok_merge is not None:
                W("T", sV, tok_merge)
            for h in range(2):
                for kc in range(8):
                    ins = nc.tensor.matmul(
                        psO[:, h * 512:(h + 1) * 512], pt2[pp][:, kc, :],
                        vg[pgp][:, kc, h * 512:(h + 1) * 512],
                        start=(kc == 0), stop=(kc == 7))
                tT(ins)
            free_ktg[pgp] = sT.v
            free_vg[pgp] = sT.v
            W("V", sT)
            tok_merge = tV(nc.vector.scalar_tensor_tensor(
                oacc[:, psub, :], oacc[:, psub, :], alpha[:, psub:psub + 1],
                psO[:], ALU.mult, ALU.add))

        for j in range(8):
            jp = j % 2
            ngrp = NJ[j] // 2
            if j < 7:
                if free_qt[(j + 1) % 2] is not None:
                    W("L", sT, free_qt[(j + 1) % 2])
                dma("L", dQT[(j + 1) % 2], qt[(j + 1) % 2][:],
                    QT_D[:, :, (j + 1) * 512:(j + 2) * 512])
            W("V", sS)
            tV(nc.vector.memset(mrun[0][:], NEG_BIG))

            for g in range(ngrp):
                gp = gctr % 2
                if free_ktg[gp] is not None:
                    W("L", sT, free_ktg[gp])
                dma("L", dKT[gp], ktg[gp][:],
                    KT_D[:, :, g * 1024:(g + 1) * 1024])
                if free_vg[gp] is not None:
                    nc.scalar.wait_ge(sT.h, free_vg[gp])
                dma("S", dVG[gp], vg[gp][:],
                    V_D[g * 1024:(g + 1) * 1024, :]
                    .rearrange("(kc p) f -> p kc f", p=128))

                ml_old = mrun[g % 2]
                ml_new = mrun[(g + 1) % 2]
                for sub in range(4):
                    pp = sub % 2
                    W("T", dKT[gp])
                    if g == 0 and sub == 0:
                        W("T", dQT[jp])
                    if tok_mask2[pp] is not None:
                        W("T", sV, tok_mask2[pp])
                    for t in range(2):
                        for fc in range(8):
                            ins = nc.tensor.matmul(
                                psS2[pp][:, t * 512:(t + 1) * 512],
                                qt[jp][:, fc, sub * 128:(sub + 1) * 128],
                                ktg[gp][:, fc, t * 512:(t + 1) * 512],
                                start=(fc == 0), stop=(fc == 7))
                        tT(ins)
                    W("V", sT)
                    colp = 2 * j * (j + 1) + g * 4 + sub
                    nc.vector.tensor_mask_reduce(
                        ssb[:], psS2[pp][:], 0.0, mep_sb[:, colp:colp + 1],
                        1.0 / 32.0, NEG_BIG, ALU.max, accum_out=mtmp[:])
                    vspace()
                    nc.vector.tensor_max(ml_new[:, sub:sub + 1],
                                         mtmp[:], ml_old[:, sub:sub + 1])
                    vspace()
                    tok_mask2[pp] = tV(nc.vector.tensor_scalar_mul(
                        nmn[:, sub:sub + 1], ml_new[:, sub:sub + 1], -1.0))
                    W("S", sV)
                    nc.scalar.activation(alpha[:, sub:sub + 1],
                                         ml_old[:, sub:sub + 1], AF.Exp,
                                         bias=nmn[:, sub:sub + 1])
                    tok_exp = tS(nc.scalar.activation(
                        pb[:, pp * 2:pp * 2 + 2, :].rearrange("p a b -> p (a b)"),
                        ssb[:].rearrange("p a b -> p (a b)"), AF.Exp,
                        bias=nmn[:, sub:sub + 1], accum_out=psum_e[:, 0:1]))
                    W("V", sS)
                    tV(nc.vector.scalar_tensor_tensor(
                        lrun[:, sub:sub + 1], lrun[:, sub:sub + 1],
                        alpha[:, sub:sub + 1], psum_e[:, 0:1], ALU.mult, ALU.add))
                    if pendB is not None:
                        flushB(pendB)
                        pendB = None
                    if pendA is not None:
                        pendB = flushA(pendA)
                        pendA = None
                    pendA = (sub, gp, tok_exp)
                gctr += 1

            if pendB is not None:
                flushB(pendB)
                pendB = None
            if pendA is not None:
                flushB(flushA(pendA))
                pendA = None
            for sub in range(4):
                rows = j * 512 + sub * 128
                xp = sub % 2
                if j > 0 or sub > 1:
                    W("L", sV)
                dma("L", dXR[xp], xres[xp][:], X_OWN[rows:rows + 128, :])
                W("V", sS)
                nc.vector.reciprocal(rl[:], lrun[:, sub:sub + 1])
                tV(nc.vector.tensor_copy(dmy2[:], lrun[:, 0:1]))
                W("S", sV)
                tS(nc.scalar.activation(onb[:], oacc[:, sub, :], AF.Identity,
                                        scale=rl[:]))
                W("T", sS)
                if tok_ptcopy[0] is not None:
                    W("T", sS, tok_ptcopy[0])
                for k in range(8):
                    ins = nc.tensor.transpose(psPT2[0][:, k, :],
                                              onb[:, k * 128:(k + 1) * 128],
                                              ident_b[:])
                tT(ins)
                W("S", sT)
                tok_ptcopy[0] = tS(nc.scalar.copy(ont[:], psPT2[0][:]))
                W("T", sS)
                if tok_merge is not None:
                    W("T", sV, tok_merge)
                for h in range(2):
                    for fc in range(8):
                        nc.tensor.matmul(
                            psO[:, h * 512:(h + 1) * 512], ont[:, fc, :],
                            wproj_b[:, fc, h * 512:(h + 1) * 512],
                            start=(fc == 0), stop=False)
                    tT(nc.tensor.matmul(psO[:, h * 512:(h + 1) * 512], ones_b[:],
                                        bp_b[:, h * 512:(h + 1) * 512],
                                        start=False, stop=True))
                W("V", sT)
                W("V", dXR[xp])
                if part_tok[xp] is not None:
                    W("V", dPART[xp])
                tok_merge = tV(nc.vector.tensor_add(part[xp][:], xres[xp][:],
                                                    psO[:]))
                W("g", sV)
                dma("g", dPART[xp], PART_D[rows:rows + 128, :], part[xp][:])
                part_tok[xp] = dPART[xp].v
            free_qt[jp] = sT.v

        barrier()
        es2.close()

    if 3 in phases:
        es3 = ExitStack()
        sb3 = lambda n, s, d=F32: es3.enter_context(nc.sbuf_tensor(n, s, d))

        wgt = sb3("wgt", [128, 8, II], BF16)
        wut = sb3("wut", [128, 8, II], BF16)
        wdt = sb3("wdt", [128, 22, C], BF16)
        xqt = [sb3(f"xqt{i}", [128, 8, 512], BF16) for i in range(2)]
        sg = sb3("sg", [128, 512])
        hb = sb3("hb", [128, 512])
        hq = sb3("hq", [128, 512], BF16)
        hqt = sb3("hqt", [128, 4, 128], BF16)
        qb3 = {"tq": sb3("p3_tq", [128, 512]), "ql": sb3("p3_ql", [128, 512]),
               "qg": sb3("p3_qg", [128, 512]), "bm": sb3("p3_bm", [128, 32]),
               "bmr": sb3("p3_bmr", [128, 32]), "rr": sb3("p3_rr", [128, 32]),
               "sc": sb3("p3_sc", [128, 32]), "dmy": sb3("p3_dmy", [128, 1])}
        pres = [sb3(f"pres{i}", [128, C]) for i in range(2)]
        outb = [sb3(f"outb{i}", [128, C]) for i in range(2)]

        psG = [es3.enter_context(nc.psum_tensor(f"psG{i}", [128, 512], F32))
               for i in range(2)]
        psU = [es3.enter_context(nc.psum_tensor(f"psU{i}", [128, 512], F32))
               for i in range(2)]
        psHT = es3.enter_context(nc.psum_tensor("psHT", [128, 4, 128], BF16))
        psD = es3.enter_context(nc.psum_tensor("psD", [128, 1024], F32))

        dW3 = mkc("dW3")
        dXQT = [mkc("dXQT0"), mkc("dXQT1")]
        dPR = [mkc("dPR0"), mkc("dPR1")]
        dOUT = [mkc("dOUT0"), mkc("dOUT1")]
        dma("L", dW3, wgt[:], WGT_D[:])
        dma("L", dW3, wut[:], WUT_D[:])
        dma("L", dW3, wdt[:], WDT_D[:])

        ic_w = [512, 512, 512, 512, 512, 256]
        tokG = [None, None]
        tokU = [None, None]
        tok_hqt = None
        hq_free = None
        tok_out = None
        free_xqt = [None, None]
        free_pres = [None, None]
        free_outb = [True, True]

        for tt in range(32):
            tp = (tt // 4) % 2
            ts4 = tt % 4
            if ts4 == 0:
                if free_xqt[tp] is not None:
                    W("L", sT, free_xqt[tp])
                dma("L", dXQT[tp], xqt[tp][:],
                    XQT_D[:, :, tt * 128:(tt + 4) * 128])
            if free_pres[tp] is not None:
                nc.scalar.wait_ge(sV.h, free_pres[tp])
            dma("S", dPR[tp], pres[tp][:], PART_D[tt * 128:(tt + 1) * 128, :])

            def emit_tail(pic, pwic, ptokq):
                nonlocal tok_hqt, hq_free
                ntp = pwic // 128
                W("T", sV, ptokq)
                if tok_hqt is not None:
                    W("T", sS, tok_hqt)
                for k in range(ntp):
                    ins = nc.tensor.transpose(psHT[:, k, :],
                                              hq[:, k * 128:(k + 1) * 128],
                                              ident_b[:])
                tT(ins)
                hq_free = sT.v
                W("S", sT)
                tok_hqt = tS(nc.scalar.copy(hqt[:, 0:ntp, :], psHT[:, 0:ntp, :]))
                W("T", sS, tok_hqt)
                if pic == 0 and tok_out is not None:
                    W("T", sV, tok_out)
                for qc in range(ntp):
                    for h in range(2):
                        ins = nc.tensor.matmul(
                            psD[:, h * 512:(h + 1) * 512], hqt[:, qc, :],
                            wdt[:, pic * 4 + qc, h * 512:(h + 1) * 512],
                            start=(pic == 0 and qc == 0),
                            stop=(pic == 5 and qc == ntp - 1))
                tT(ins)

            i0 = 0
            pend = None
            for ic in range(6):
                wic = ic_w[ic]
                gp = ic % 2
                W("T", dW3)
                W("T", dXQT[tp])
                if tokG[gp] is not None:
                    W("T", sS, tokG[gp])
                if tokU[gp] is not None:
                    W("T", sV, tokU[gp])
                for fc in range(8):
                    ins = nc.tensor.matmul(
                        psG[gp][:, 0:wic],
                        xqt[tp][:, fc, ts4 * 128:(ts4 + 1) * 128],
                        wgt[:, fc, i0:i0 + wic],
                        start=(fc == 0), stop=(fc == 7))
                tT(ins)
                tok_gmm = sT.v
                for fc in range(8):
                    ins = nc.tensor.matmul(
                        psU[gp][:, 0:wic],
                        xqt[tp][:, fc, ts4 * 128:(ts4 + 1) * 128],
                        wut[:, fc, i0:i0 + wic],
                        start=(fc == 0), stop=(fc == 7))
                tT(ins)
                tok_umm = sT.v
                if pend is not None:
                    emit_tail(*pend)
                W("S", sT, tok_gmm)
                tokG[gp] = tS(nc.scalar.activation(sg[:, 0:wic], psG[gp][:, 0:wic],
                                                   AF.Silu))
                W("V", sS, tokG[gp])
                W("V", sT, tok_umm)
                if hq_free is not None:
                    W("V", sT, hq_free)
                tokU[gp] = tV(nc.vector.tensor_tensor(hb[:, 0:wic], sg[:, 0:wic],
                                                      psU[gp][:, 0:wic], ALU.mult))
                tokq = tV(emit_qdq(hb[:, 0:wic], hq[:, 0:wic], qb3, wic))
                pend = (ic, wic, tokq)
                i0 += wic
            emit_tail(*pend)
            pend = None
            if ts4 == 3:
                free_xqt[tp] = sT.v
            W("V", sT)
            W("V", dPR[tp])
            if dOUT[tp].v:
                W("V", dOUT[tp])
            tok_out = tV(nc.vector.tensor_add(outb[tp][:], pres[tp][:],
                                              psD[:]))
            free_pres[tp] = tok_out
            W("g", sV)
            dma("g", dOUT[tp], OUT[tt * 128:(tt + 1) * 128, :], outb[tp][:])

        nc.gpsimd.wait_ge(dOUT[0].h, dOUT[0].v)
        nc.gpsimd.wait_ge(dOUT[1].h, dOUT[1].v)
        es3.close()

    es_global.close()
    return nc



def _build_me(ty):
    me = np.zeros((128, NCHUNK_COLS // 2), dtype=np.float32)
    p = np.arange(128, dtype=np.float32)
    col = 0
    for j in range(8):
        g = GS[ty][j]
        for gg in range(NJ[j] // 2):
            for sub in range(4):
                me[:, col] = np.maximum(
                    (512.0 * g + 128.0 * sub + p + 1.0) - 1024.0 * gg, 0.0)
                col += 1
    assert col == NCHUNK_COLS // 2
    return me


def kernel(L0_attn_x, L0_attn_w_attn, L0_attn_b_attn, L0_attn_w_proj,
           L0_attn_b_proj, L0_ffn_w_gate, L0_ffn_w_up, L0_ffn_w_down):
    x = np.ascontiguousarray(np.asarray(L0_attn_x, dtype=np.float32))
    wattn = np.ascontiguousarray(np.asarray(L0_attn_w_attn, dtype=np.float32))
    battn = np.ascontiguousarray(np.asarray(L0_attn_b_attn, dtype=np.float32))
    wproj = np.ascontiguousarray(np.asarray(L0_attn_w_proj, dtype=np.float32))
    bproj = np.ascontiguousarray(np.asarray(L0_attn_b_proj, dtype=np.float32))
    wg = np.ascontiguousarray(np.asarray(L0_ffn_w_gate, dtype=np.float32))
    wu = np.ascontiguousarray(np.asarray(L0_ffn_w_up, dtype=np.float32))
    wd = np.ascontiguousarray(np.asarray(L0_ffn_w_down, dtype=np.float32))
    ident = np.eye(128, dtype=np.float32).astype(ml_dtypes.bfloat16)
    battn_pm = np.ascontiguousarray(battn.reshape(24, 128).T)
    mes = [_build_me(0), _build_me(1)]

    in_maps = []
    for core in range(8):
        s, ty = core // 2, core % 2
        x_own = np.ascontiguousarray(
            np.concatenate([x[s][512 * g:512 * (g + 1)] for g in GS[ty]], axis=0))
        in_maps.append({
            "x_seq": x[s], "x_own": x_own, "me": mes[ty], "ident": ident,
            "wattn": wattn, "battn": battn, "battn_pm": battn_pm,
            "wproj": wproj, "bproj": bproj,
            "wg": wg, "wu": wu, "wd": wd,
        })

    nc = build_program()
    mybir.codegen_inst_isa_subclasses(nc)
    res = run_bass_kernel_spmd(nc, in_maps, core_ids=list(range(8)))

    out = np.empty((B, T, C), dtype=np.float32)
    for core in range(8):
        s, ty = core // 2, core % 2
        oo = np.asarray(res.results[core]["out_own"])
        for j, g in enumerate(GS[ty]):
            out[s][512 * g:512 * (g + 1)] = oo[512 * j:512 * (j + 1)]
    return out
